# version 02
# baseline (speedup 1.0000x reference)
import contextlib
from collections import defaultdict

import numpy as np
import ml_dtypes

import concourse.bass as bass
import concourse.mybir as mybir
from concourse.bass_utils import run_bass_kernel_spmd

F32 = mybir.dt.float32
BF16 = mybir.dt.bfloat16
AF = mybir.ActivationFunctionType
ALU = mybir.AluOpType
AX = mybir.AxisListType.X

S, D = 2048, 1024
TT, KC = 16, 8
NH, DK = 16, 64
NE, IM, ISH = 64, 512, 2048
NU = NE + 4
MBIAS = 30.0

ENGINES = ("sync", "tensor", "vector", "scalar", "gpsimd")


SELF_SEM = {"vector": "V", "scalar": "S", "tensor": "T", "gpsimd": "G"}


class Rec:
    def __init__(self):
        self.q = {e: [] for e in ENGINES}
        self.cnt = defaultdict(int)

    def op(self, eng, fn, waits=(), inc=None, n=1, guard=False):
        waits = [w for w in waits if w and w[1]]
        if guard:
            for ent in reversed(self.q[eng]):
                if ent[1] is not None:
                    if ent[2] is None:
                        s = SELF_SEM[eng]
                        self.cnt[s] += 1
                        ent[2] = s
                        ent[3] = 1
                        ent[4] = self.cnt[s]
                    waits.append((ent[2], ent[4]))
                    break
        post = None
        if inc is not None:
            self.cnt[inc] += n
            post = self.cnt[inc]
        self.q[eng].append([tuple(waits), fn, inc, n, post])
        return post

    def wait(self, eng, waits):
        self.q[eng].append([tuple(w for w in waits if w and w[1]), None, None, 0, None])

    def replay(self, nc, sems):
        with nc.Block() as block:
            for name in ENGINES:
                def body(eng, name=name):
                    seen = {}
                    for waits, fn, inc, n, _post in self.q[name]:
                        for s, v in waits:
                            if seen.get(s, -1) < v:
                                eng.wait_ge(sems[s], v)
                                seen[s] = v
                        if fn is None:
                            continue
                        inst = fn(eng)
                        if inc is not None:
                            inst.then_inc(sems[inc], n)
                getattr(block, name)(body)


def build_nc(dbg=(), verbose=False):
    nc = bass.Bass("TRN2", target_bir_lowering=False,
                   detect_race_conditions=False)
    t = {}
    t["x"] = nc.dram_tensor("x", [S, D], F32, kind="ExternalInput").ap()
    t["wqkv"] = nc.dram_tensor("wqkv", [3, NH, DK, D], F32, kind="ExternalInput").ap()
    t["wo"] = nc.dram_tensor("wo", [D, D], F32, kind="ExternalInput").ap()
    t["rw"] = nc.dram_tensor("rw", [NE, D], F32, kind="ExternalInput").ap()
    t["rb"] = nc.dram_tensor("rb", [NE], F32, kind="ExternalInput").ap()
    t["upw"] = nc.dram_tensor("upw", [NE, D, IM], F32, kind="ExternalInput").ap()
    t["downw"] = nc.dram_tensor("downw", [NE, IM, D], F32, kind="ExternalInput").ap()
    t["shup"] = nc.dram_tensor("shup", [D, ISH], F32, kind="ExternalInput").ap()
    t["shdn"] = nc.dram_tensor("shdn", [ISH, D], F32, kind="ExternalInput").ap()
    t["y"] = nc.dram_tensor("y", [S, D], F32, kind="ExternalOutput").ap()

    dbg_t = {}
    for name, shape, dt in (
        ("dbg_xnT", [128, KC, S], BF16),
        ("dbg_xnloT", [128, KC, S], BF16),
        ("dbg_c", [128, TT, NE], F32),
        ("dbg_attn", [128, KC, S], BF16),
        ("dbg_rs", [NH, S], F32),
        ("dbg_Y", [128, TT, D], F32),
        ("dbg_sfc", [128, TT, NE], F32),
        ("dbg_lg", [128, TT, NE], F32),
        ("dbg_gsc", [128, TT, 8], F32),
        ("dbg_gmask", [128, TT, 8], F32),
        ("dbg_selm", [128, TT, NE], F32),
    ):
        if name in dbg:
            dbg_t[name] = nc.dram_tensor(name, shape, dt, kind="ExternalOutput").ap()

    ident_dr = nc.inline_tensor(np.eye(128, dtype=ml_dtypes.bfloat16), "identc")
    mask_np = np.where(np.tril(np.ones((128, 128), np.float32)) > 0, 0.0, -1e30
                       ).astype(np.float32)
    mask_dr = nc.inline_tensor(mask_np, "maskc")
    sel_np = np.zeros((2, 128), dtype=ml_dtypes.bfloat16)
    sel_np[0, 0:64] = 1.0
    sel_np[1, 64:128] = 1.0
    sel_dr = nc.inline_tensor(sel_np, "selc")

    rec = Rec()
    C = rec.cnt

    def log_space(tag):
        if verbose:
            print(f"[sbuf] {tag}: {nc.sbuf_bytes_remaining} bytes/part free, "
                  f"{nc.psum_banks_remaining} psum banks free")

    est = contextlib.ExitStack()
    with est:
        ident = est.enter_context(nc.sbuf_tensor("ident", [128, 128], BF16))
        maskA = est.enter_context(nc.sbuf_tensor("maskA", [128, 128], F32))
        sel2 = est.enter_context(nc.sbuf_tensor("sel2", [2, 128], BF16))
        mb = est.enter_context(nc.sbuf_tensor("mb", [128, 1], F32))
        epsb = est.enter_context(nc.sbuf_tensor("epsb", [128, 1], F32))
        xnT = est.enter_context(nc.sbuf_tensor("xnT", [128, KC, S], BF16))
        ctile = est.enter_context(nc.sbuf_tensor("ctile", [128, TT, NE], F32))
        brep = est.enter_context(nc.sbuf_tensor("brep", [128, NE], F32))
        rsums = est.enter_context(nc.sbuf_tensor("rsums", [NH, S], F32))
        Y = est.enter_context(nc.sbuf_tensor("Y", [128, TT, D], F32))

        rec.op("sync", lambda e: e.dma_start(out=ident[:], in_=ident_dr.ap()), inc="Q", n=16)
        rec.op("sync", lambda e: e.dma_start(out=maskA[:], in_=mask_dr.ap()), inc="Q", n=16)
        rec.op("sync", lambda e: e.dma_start(out=sel2[:], in_=sel_dr.ap()), inc="Q", n=16)
        q_consts = C["Q"]
        rec.op("vector", lambda e: e.memset(mb[:], -MBIAS))
        v_mb = rec.op("vector", lambda e: e.memset(epsb[:], 1e-5), inc="V")

        stA = contextlib.ExitStack()
        with stA:
            xrot = stA.enter_context(nc.sbuf_tensor("xrot", [128, 4, D], F32))
            xsq = stA.enter_context(nc.sbuf_tensor("xsq", [128, 2, D], F32))
            xnbf = stA.enter_context(nc.sbuf_tensor("xnbf", [128, 2, D], BF16))
            xn32 = stA.enter_context(nc.sbuf_tensor("xn32", [128, 2, D], F32))
            xnlo = stA.enter_context(nc.sbuf_tensor("xnlo", [128, 2, D], BF16))
            xnloT = stA.enter_context(nc.sbuf_tensor("xnloT", [128, KC, S], BF16))
            sm = {k: stA.enter_context(nc.sbuf_tensor("st_" + k, [128, 2], F32))
                  for k in ("msum", "ssq", "mean", "e2", "msq", "var", "sd",
                            "rstd", "nm", "nmneg")}
            rwst = stA.enter_context(nc.sbuf_tensor("rwst", [64, D], F32))
            rwhi = stA.enter_context(nc.sbuf_tensor("rwhi", [64, D], BF16))
            rwhi32 = stA.enter_context(nc.sbuf_tensor("rwhi32", [64, D], F32))
            rwlo = stA.enter_context(nc.sbuf_tensor("rwlo", [64, D], BF16))
            rwT = stA.enter_context(nc.sbuf_tensor("rwT", [128, KC, 2, 64], BF16))
            ps_tr = stA.enter_context(nc.psum_tensor("ps_tr", [128, 4, 1024], BF16))
            log_space("phase A")

            rec.op("sync", lambda e: e.dma_start(out=rwst[:], in_=t["rw"]), inc="Q", n=16)
            q_rw = C["Q"]
            rec.op("sync", lambda e: e.dma_start(out=brep[0:1, :], in_=t["rb"].unsqueeze(0)),
                   inc="Q", n=16)
            np_ = 1
            while np_ < 128:
                w = [("Q", C["Q"])]
                rec.op("sync", lambda e, np_=np_: e.dma_start(
                    out=brep[np_:2 * np_, :], in_=brep[0:np_, :]), waits=w, inc="Q", n=16)
                np_ *= 2
            q_brep = q_pre = C["Q"]

            qx, s_lo32, v_ms, v_ssq, s_sq = {}, {}, {}, {}, {}
            s_sd_d = {}
            t_hi, t_lo, v_lo = {}, {}, {}
            tr_evac = {0: 0, 1: 0, 2: 0, 3: 0}
            tr_i = [0]

            def transpose_to(src_ap_fn, dst_ap_fn, wfirst):
                sl = tr_i[0] % 4
                w = list(wfirst) + ([("V", tr_evac[sl])] if tr_evac[sl] else [])
                tp = rec.op("tensor", lambda e, sl=sl: e.transpose(
                    ps_tr[:, sl, 0:128], src_ap_fn(), ident[:]), waits=w, inc="T")
                tr_evac[sl] = rec.op("vector", lambda e, sl=sl: dst_ap_fn(e, ps_tr[:, sl, 0:128]),
                                     waits=[("T", tp)], inc="V")
                tr_i[0] += 1
                return tp

            for i in range(TT):
                p2, p4 = i % 2, i % 4
                w = []
                if i >= 4:
                    w = [("S", s_lo32[i - 4]), ("V", v_ms[i - 4])]
                qx[i] = rec.op("sync", lambda e, i=i, p4=p4: e.dma_start(
                    out=xrot[:, p4, :], in_=t["x"][128 * i:128 * (i + 1), :]),
                    waits=w, inc="QX%d" % p4, n=16)
                w = [("QX%d" % p4, qx[i])]
                if i >= 2:
                    w.append(("V", v_ssq[i - 2]))
                s_sq[i] = rec.op("scalar", lambda e, p2=p2, p4=p4: e.activation(
                    xsq[:, p2, :], xrot[:, p4, :], AF.Square), waits=w, inc="S")
                v_ms[i] = rec.op("vector", lambda e, p2=p2, p4=p4: e.reduce_sum(
                    sm["msum"][:, p2:p2 + 1], xrot[:, p4, :], axis=AX),
                    waits=[("QX%d" % p4, qx[i])], inc="V")
                v_ssq[i] = rec.op("vector", lambda e, p2=p2: e.reduce_sum(
                    sm["ssq"][:, p2:p2 + 1], xsq[:, p2, :], axis=AX),
                    waits=[("S", s_sq[i])], inc="V")
                cc = lambda k, p2=p2: sm[k][:, p2:p2 + 1]
                rec.op("vector", lambda e, cc=cc: e.tensor_scalar_mul(cc("mean"), cc("msum"), 1.0 / D))
                rec.op("vector", lambda e, cc=cc: e.tensor_scalar_mul(cc("e2"), cc("ssq"), 1.0 / D))
                rec.op("vector", lambda e, cc=cc: e.tensor_tensor(cc("msq"), cc("mean"), cc("mean"), ALU.mult))
                w = [("S", s_sd_d[i - 2])] if i >= 2 else []
                v_var = rec.op("vector", lambda e, cc=cc: e.tensor_tensor(
                    cc("var"), cc("e2"), cc("msq"), ALU.subtract), waits=w, inc="V",
                    guard=True)
                s_sd = s_sd_d[i] = rec.op("scalar", lambda e, cc=cc: e.activation(
                    cc("sd"), cc("var"), AF.Sqrt, bias=epsb[:]),
                    waits=[("V", v_var), ("V", v_mb)], inc="S")
                w = [("S", s_sd)]
                if i >= 2:
                    w.append(("S", s_lo32[i - 2]))
                rec.op("vector", lambda e, cc=cc: e.reciprocal(cc("rstd"), cc("sd")),
                       waits=w)
                rec.op("vector", lambda e, cc=cc: e.tensor_tensor(
                    cc("nm"), cc("mean"), cc("rstd"), ALU.mult), guard=True)
                v_stat = rec.op("vector", lambda e, cc=cc: e.tensor_scalar_mul(
                    cc("nmneg"), cc("nm"), -1.0), inc="V", guard=True)
                w = [("V", v_stat)]
                if i >= 2:
                    w += [("T", t_hi[i - 2]), ("V", v_lo[i - 2])]
                rec.op("scalar", lambda e, p2=p2, p4=p4, cc=cc: e.activation(
                    xnbf[:, p2, :], xrot[:, p4, :], AF.Identity,
                    bias=cc("nmneg"), scale=cc("rstd")), waits=w, inc="S")
                w = [("T", t_lo[i - 2])] if i >= 2 else []
                s_lo32[i] = rec.op("scalar", lambda e, p2=p2, p4=p4, cc=cc: e.activation(
                    xn32[:, p2, :], xrot[:, p4, :], AF.Identity,
                    bias=cc("nmneg"), scale=cc("rstd")), waits=w, inc="S")
                v_lo[i] = rec.op("vector", lambda e, p2=p2: e.tensor_tensor(
                    xnlo[:, p2, :], xn32[:, p2, :], xnbf[:, p2, :], ALU.subtract),
                    waits=[("S", s_lo32[i])], inc="V")

                for k in range(KC):
                    wf = [("S", s_lo32[i]), ("Q", q_pre)] if k == 0 else []
                    transpose_to(
                        lambda p2=p2, k=k: xnbf[:, p2, 128 * k:128 * (k + 1)],
                        lambda e, ps, k=k, i=i: e.tensor_copy(
                            xnT[:, k, 128 * i:128 * (i + 1)], ps), wf)
                t_hi[i] = C["T"]
                for k in range(KC):
                    wf = [("V", v_lo[i])] if k == 0 else []
                    transpose_to(
                        lambda p2=p2, k=k: xnlo[:, p2, 128 * k:128 * (k + 1)],
                        lambda e, ps, k=k, i=i: e.tensor_copy(
                            xnloT[:, k, 128 * i:128 * (i + 1)], ps), wf)
                t_lo[i] = C["T"]

            v_rwhi = rec.op("vector", lambda e: e.tensor_copy(rwhi[:], rwst[:]),
                            waits=[("Q", q_pre)], inc="V")
            s_rwhi32 = rec.op("scalar", lambda e: e.copy(rwhi32[:], rwhi[:]),
                              waits=[("V", v_rwhi)], inc="S")
            v_rwlo = rec.op("vector", lambda e: e.tensor_tensor(
                rwlo[:], rwst[:], rwhi32[:], ALU.subtract),
                waits=[("S", s_rwhi32)], inc="V")
            for hl, src in ((0, rwhi), (1, rwlo)):
                for k in range(KC):
                    wf = [("V", v_rwlo)] if (k == 0 and hl == 0) else []
                    sl = tr_i[0] % 4
                    w = list(wf) + ([("V", tr_evac[sl])] if tr_evac[sl] else [])
                    tp = rec.op("tensor", lambda e, sl=sl, src=src, k=k: e.transpose(
                        ps_tr[:, sl, 0:64], src[:, 128 * k:128 * (k + 1)],
                        ident[0:64, 0:64]), waits=w, inc="T")
                    tr_evac[sl] = rec.op("vector", lambda e, sl=sl, hl=hl, k=k: e.tensor_copy(
                        rwT[:, k, hl, :], ps_tr[:, sl, 0:64]),
                        waits=[("T", tp)], inc="V")
                    tr_i[0] += 1

            if "dbg_xnT" in dbg_t:
                rec.op("sync", lambda e: e.dma_start(out=dbg_t["dbg_xnT"], in_=xnT[:]),
                       waits=[("V", C["V"])], inc="QO", n=16)
            if "dbg_xnloT" in dbg_t:
                rec.op("sync", lambda e: e.dma_start(out=dbg_t["dbg_xnloT"], in_=xnloT[:]),
                       waits=[("V", C["V"])], inc="QO", n=16)

            vA, sA, tA = C["V"], C["S"], C["T"]

            stB = contextlib.ExitStack()
            with stB:
                lg_ps = stB.enter_context(nc.psum_tensor("lg_ps", [128, 2, 512], F32))
                nm_ = lambda k, sh=(128, NE): stB.enter_context(
                    nc.sbuf_tensor("rb_" + k, [128, sh[1]] if len(sh) == 2 else list(sh), F32))
                sc = stB.enter_context(nc.sbuf_tensor("rb_sc", [128, 2, NE], F32))
                sfc = nm_("sfc")
                eq1 = nm_("eq1")
                grpt = nm_("grpt")
                msk = nm_("msk")
                selm = nm_("selm")
                eqt = nm_("eqt")
                tw = nm_("tw")
                m1 = stB.enter_context(nc.sbuf_tensor("rb_m1", [128, 8], F32))
                m2 = stB.enter_context(nc.sbuf_tensor("rb_m2", [128, 8], F32))
                gsc = stB.enter_context(nc.sbuf_tensor("rb_gsc", [128, 8], F32))
                gmask = stB.enter_context(nc.sbuf_tensor("rb_gmask", [128, 8], F32))
                eqg = stB.enter_context(nc.sbuf_tensor("rb_eqg", [128, 8], F32))
                gmx = stB.enter_context(nc.sbuf_tensor("rb_gmx", [128, 1], F32))
                mx = stB.enter_context(nc.sbuf_tensor("rb_mx", [128, 1], F32))
                ssum = stB.enter_context(nc.sbuf_tensor("rb_ssum", [128, 1], F32))
                fac = stB.enter_context(nc.sbuf_tensor("rb_fac", [128, 1], F32))
                log_space("phase B")

                if "dbg_sfc" in dbg_t:
                    dbg_rt = stB.enter_context(
                        nc.sbuf_tensor("dbg_rt", [128, TT, 208], BF16))
                s_sig, v_c, v_dbglg = {}, {}, {}
                g8 = lambda ap: ap.rearrange("p (g i) -> p g i", g=8)
                for i in range(TT):
                    p2 = i % 2
                    groups = [(xnT, 0), (xnloT, 0), (xnT, 1)]
                    for gi, (at, hl) in enumerate(groups):
                        for k in range(KC):
                            w = []
                            if gi == 0 and k == 0:
                                w = [("V", vA)]
                                if i >= 2:
                                    w.append(("S", s_sig[i - 2]))
                                    if v_dbglg.get(i - 2):
                                        w.append(("V", v_dbglg[i - 2]))
                            st = (gi == 0 and k == 0)
                            sp_ = (gi == 2 and k == KC - 1)
                            t_lg = rec.op("tensor",
                                lambda e, at=at, hl=hl, k=k, i=i, p2=p2, st=st, sp_=sp_:
                                e.matmul(lg_ps[:, p2, 0:64],
                                         at[:, k, 128 * i:128 * (i + 1)],
                                         rwT[:, k, hl, :], start=st, stop=sp_),
                                waits=w, inc="T" if sp_ else None)
                    if "dbg_sfc" in dbg_t:
                        v_dbglg[i] = rec.op("vector", lambda e, i=i, p2=p2: e.tensor_copy(
                            dbg_rt[:, i, 144:208], lg_ps[:, p2, 0:64]),
                            waits=[("T", t_lg)], inc="V")
                    w = [("T", t_lg)]
                    if i >= 2:
                        w.append(("V", v_c[i - 2]))
                    s_sig[i] = rec.op("scalar", lambda e, p2=p2: e.activation(
                        sc[:, p2, :], lg_ps[:, p2, 0:64], AF.Sigmoid),
                        waits=w, inc="S")
                    rec.op("vector", lambda e, p2=p2: e.tensor_tensor(
                        sfc[:], sc[:, p2, :], brep[:], ALU.add),
                        waits=[("S", s_sig[i]), ("Q", q_brep)])
                    if "dbg_sfc" in dbg_t:
                        rec.op("vector", lambda e, i=i: e.tensor_copy(
                            dbg_rt[:, i, 0:64], sfc[:]))
                    rec.op("vector", lambda e: e.reduce_max(
                        m1[:].unsqueeze(2), g8(sfc[:]), axis=AX), guard=True)
                    rec.op("vector", lambda e: e.tensor_tensor(
                        g8(eq1[:]), g8(sfc[:]),
                        m1[:].unsqueeze(2).broadcast_to((128, 8, 8)), ALU.is_equal),
                        guard=True)
                    rec.op("vector", lambda e: e.scalar_tensor_tensor(
                        grpt[:], eq1[:], -1e30, sfc[:], ALU.mult, ALU.add), guard=True)
                    rec.op("vector", lambda e: e.reduce_max(
                        m2[:].unsqueeze(2), g8(grpt[:]), axis=AX), guard=True)
                    rec.op("vector", lambda e: e.tensor_tensor(
                        gsc[:], m1[:], m2[:], ALU.add), guard=True)
                    if "dbg_sfc" in dbg_t:
                        rec.op("vector", lambda e, i=i: e.tensor_copy(
                            dbg_rt[:, i, 64:72], gsc[:]))
                    rec.op("vector", lambda e: e.memset(gmask[:], 0.0))
                    for r in range(4):
                        rec.op("vector", lambda e: e.reduce_max(
                            gmx[:], gsc[:], axis=AX), guard=True)
                        rec.op("vector", lambda e: e.tensor_tensor(
                            eqg[:], gsc[:], gmx[:].broadcast_to((128, 8)), ALU.is_equal),
                            guard=True)
                        rec.op("vector", lambda e: e.tensor_tensor(
                            gmask[:], gmask[:], eqg[:], ALU.add), guard=True)
                        if r < 3:
                            rec.op("vector", lambda e: e.scalar_tensor_tensor(
                                gsc[:], eqg[:], -1e30, gsc[:], ALU.mult, ALU.add))
                    if "dbg_sfc" in dbg_t:
                        rec.op("vector", lambda e, i=i: e.tensor_copy(
                            dbg_rt[:, i, 72:80], gmask[:]))
                    rec.op("vector", lambda e: e.tensor_tensor(
                        g8(msk[:]), g8(sfc[:]),
                        gmask[:].unsqueeze(2).broadcast_to((128, 8, 8)), ALU.mult),
                        guard=True)
                    for r in range(8):
                        rec.op("vector", lambda e: e.reduce_max(
                            mx[:], msk[:], axis=AX), guard=True)
                        rec.op("vector", lambda e: e.tensor_tensor(
                            eqt[:], msk[:], mx[:].broadcast_to((128, NE)), ALU.is_equal),
                            guard=True)
                        if r == 0:
                            rec.op("vector", lambda e: e.tensor_copy(
                                selm[:], eqt[:]), guard=True)
                        else:
                            rec.op("vector", lambda e: e.tensor_tensor(
                                selm[:], selm[:], eqt[:], ALU.add), guard=True)
                        if r < 7:
                            rec.op("vector", lambda e: e.scalar_tensor_tensor(
                                msk[:], eqt[:], -1e30, msk[:], ALU.mult, ALU.add))
                    if "dbg_sfc" in dbg_t:
                        rec.op("vector", lambda e, i=i: e.tensor_copy(
                            dbg_rt[:, i, 80:144], selm[:]))
                    rec.op("vector", lambda e, p2=p2: e.tensor_tensor(
                        tw[:], sc[:, p2, :], selm[:], ALU.mult), guard=True)
                    rec.op("vector", lambda e: e.reduce_sum(
                        ssum[:], tw[:], axis=AX), guard=True)
                    rec.op("vector", lambda e: e.tensor_scalar_add(
                        ssum[:], ssum[:], 1e-20), guard=True)
                    rec.op("vector", lambda e: e.reciprocal(fac[:], ssum[:]), guard=True)
                    rec.op("vector", lambda e: e.tensor_scalar_mul(
                        fac[:], fac[:], 2.5), guard=True)
                    v_c[i] = rec.op("vector", lambda e, i=i: e.tensor_tensor(
                        ctile[:, i, :], tw[:], fac[:].broadcast_to((128, NE)), ALU.mult),
                        inc="V", guard=True)

                if "dbg_sfc" in dbg_t:
                    for nm2, c0, c1 in (("dbg_sfc", 0, 64), ("dbg_gsc", 64, 72),
                                        ("dbg_gmask", 72, 80), ("dbg_selm", 80, 144),
                                        ("dbg_lg", 144, 208)):
                        rec.op("gpsimd", lambda e, nm2=nm2, c0=c0, c1=c1: e.dma_start(
                            out=dbg_t[nm2], in_=dbg_rt[:, :, c0:c1]),
                            waits=[("V", C["V"])], inc="GO", n=16)
                if "dbg_c" in dbg_t:
                    rec.op("sync", lambda e: e.dma_start(out=dbg_t["dbg_c"], in_=ctile[:]),
                           waits=[("V", C["V"])], inc="QO", n=16)
                vB, sB, tB = C["V"], C["S"], C["T"]

        stC = contextlib.ExitStack()
        with stC:
            acT = stC.enter_context(nc.sbuf_tensor("acT", [128, KC, S], BF16))
            stCa = contextlib.ExitStack()
            stCa.__enter__()
            wqst = stCa.enter_context(nc.sbuf_tensor("wqst", [128, 3, D], BF16))
            qkvwT = stCa.enter_context(nc.sbuf_tensor("qkvwT", [128, KC, 3, 128], BF16))
            QT = stCa.enter_context(nc.sbuf_tensor("QT", [128, S], BF16))
            KTb = stCa.enter_context(nc.sbuf_tensor("KTb", [128, S], BF16))
            Vau = stCa.enter_context(nc.sbuf_tensor("Vau", [128, 2, TT, 65], BF16))
            ptm = stCa.enter_context(nc.sbuf_tensor("ptm", [128, S], BF16))
            prTb = stCa.enter_context(nc.sbuf_tensor("prTb", [128, TT, 512], BF16))
            rss = stCa.enter_context(nc.sbuf_tensor("rss", [128, 2, 512], F32))
            atmp = stCa.enter_context(nc.sbuf_tensor("atmp", [64, 2, 512], BF16))
            ngmx = stCa.enter_context(nc.sbuf_tensor("ngmx", [128, 2], F32))
            ebias = stCa.enter_context(nc.sbuf_tensor("ebias", [128, 2], F32))
            ps_at = stCa.enter_context(nc.psum_tensor("ps_at", [128, 4, 512], F32))
            ps_tr2 = stCa.enter_context(nc.psum_tensor("ps_tr2", [128, 2, 1024], BF16))
            ps_pv = stCa.enter_context(nc.psum_tensor("ps_pv", [128, 1, 512], F32))
            log_space("phase C")

            q_xy = {}
            for i in range(TT):
                w = [("V", vB), ("S", sB), ("T", tB)] if i == 0 else []
                q_xy[i] = rec.op("sync", lambda e, i=i: e.dma_start(
                    out=Y[:, i, :], in_=t["x"][128 * i:128 * (i + 1), :]),
                    waits=w, inc="Q", n=16)
            q_xy_tot = C["Q"]
            v_ones = rec.op("vector", lambda e: e.memset(
                Vau[:].rearrange("p b t c -> p (b t) c")[:, :, 64:65], 1.0),
                waits=[("V", vB)], inc="V")

            g_wq, t_qkvw = {}, {}
            trc_evac = [0]
            q_rss = {0: 0, 1: 0}
            q_atmp = [0]
            v_pv_evac = [0]
            s_exp_last = [0]
            v_vau_last = [0]
            s_prT_last = [0]
            t_tr_done = {}
            v_attn_misc = [0]
            v_przero = [0]

            for r in range(8):
                w = [("V", vB), ("S", sB)] if r == 0 else [("T", t_qkvw[r - 1])]
                g_wq[r] = rec.op("gpsimd", lambda e, r=r: e.dma_start(
                    out=wqst[:],
                    in_=t["wqkv"][:, 2 * r:2 * r + 2, :, :].rearrange("q h k m -> (h k) q m")),
                    waits=w, inc="G", n=16)
                for q in range(3):
                    for k in range(KC):
                        idx = q * KC + k
                        bank = idx % 2
                        w = [("G", g_wq[r])] if idx == 0 else []
                        if r == 0 and idx == 0:
                            w.append(("Q", q_pre))
                        if trc_evac[0]:
                            w.append(("V", trc_evac[0]))
                        if s_prT_last[0]:
                            w.append(("S", s_prT_last[0]))
                        tp = rec.op("tensor", lambda e, bank=bank, q=q, k=k: e.transpose(
                            ps_tr2[:, bank, 0:128], wqst[:, q, 128 * k:128 * (k + 1)],
                            ident[:]), waits=w, inc="T")
                        trc_evac[0] = rec.op("vector", lambda e, bank=bank, q=q, k=k:
                            e.tensor_copy(qkvwT[:, k, q, :], ps_tr2[:, bank, 0:128]),
                            waits=[("T", tp)], inc="V")
                t_qkvw[r] = C["T"]
                v_qkvw_done = C["V"]

                v_qk = {}
                for qi2 in range(2):
                    tq = None
                    for k in range(KC):
                        for n in range(4):
                            w = []
                            if k == 0 and n == 0:
                                w = [("V", v_qkvw_done)]
                                if qi2 == 0:
                                    w.append(("S", s_exp_last[0]))
                                else:
                                    w.append(("V", v_qk[0]))
                            tq = rec.op("tensor", lambda e, qi2=qi2, k=k, n=n: e.matmul(
                                ps_at[:, n, :], qkvwT[:, k, qi2, :],
                                xnT[:, k, 512 * n:512 * (n + 1)],
                                start=(k == 0), stop=(k == KC - 1)),
                                waits=w, inc="T" if (k == KC - 1 and n == 3) else None)
                    dstv = QT if qi2 == 0 else KTb
                    v_qk[qi2] = rec.op("vector", lambda e, dstv=dstv: e.tensor_copy(
                        dstv[:], ps_at[:].rearrange("p n f -> p (n f)")),
                        waits=[("T", tq)], inc="V")

                v_vau = {}
                for ti in range(TT):
                    tv = None
                    for k in range(KC):
                        w = []
                        if k == 0:
                            if ti < 2:
                                w = [("V", v_qk[1]), ("V", v_ones)]
                            else:
                                w = [("V", v_vau[ti - 2])]
                        tv = rec.op("tensor", lambda e, ti=ti, k=k: e.matmul(
                            ps_at[:, ti % 2, 0:128],
                            xnT[:, k, 128 * ti:128 * (ti + 1)],
                            qkvwT[:, k, 2, :],
                            start=(k == 0), stop=(k == KC - 1)),
                            waits=w, inc="T" if k == KC - 1 else None)
                    rec.op("vector", lambda e, ti=ti: e.tensor_copy(
                        Vau[:, 0, ti, 0:64], ps_at[:, ti % 2, 0:64]),
                        waits=[("T", tv)], inc="V")
                    v_vau[ti] = rec.op("vector", lambda e, ti=ti: e.tensor_copy(
                        Vau[:, 1, ti, 0:64], ps_at[:, ti % 2, 64:128]), inc="V")
                v_vau_last[0] = v_vau[TT - 1]

                for hb in (0, 1):
                    off = 64 * hb
                    h = 2 * r + hb
                    for sp in range(4):
                        for qt in range(4):
                            qi = 4 * sp + qt
                            L = 128 * (qi + 1)
                            nb = (L + 511) // 512
                            pp = qi % 2
                            ts = None
                            for n in range(nb):
                                Nn = min(512, L - 512 * n)
                                w = []
                                if n == 0:
                                    if qi == 0 and hb == 0:
                                        w = [("V", v_vau_last[0]), ("V", v_qk[1])]
                                    else:
                                        w = [("S", s_exp_last[0])]
                                ts = rec.op("tensor", lambda e, off=off, qi=qi, n=n, Nn=Nn:
                                    e.matmul(ps_at[:, n, 0:Nn],
                                             QT[off:off + 64, 128 * qi:128 * (qi + 1)],
                                             KTb[off:off + 64, 512 * n:512 * n + Nn],
                                             start=True, stop=True,
                                             tile_position=(off, 0)),
                                    waits=w, inc="T" if n == nb - 1 else None)
                            vm = rec.op("vector", lambda e, L=L: e.tensor_tensor(
                                ps_at[:].rearrange("p n f -> p (n f)")[:, L - 128:L],
                                ps_at[:].rearrange("p n f -> p (n f)")[:, L - 128:L],
                                maskA[:], ALU.add),
                                waits=[("T", ts)], inc="V")
                            w = [("V", vm)]
                            if s_exp_last[0]:
                                w.append(("S", s_exp_last[0]))
                            rec.op("vector", lambda e, L=L, pp=pp: e.reduce_max(
                                ngmx[:, pp:pp + 1],
                                ps_at[:].rearrange("p n f -> p (n f)")[:, 0:L],
                                axis=AX, negate=True), waits=w, guard=True)
                            vb2 = rec.op("vector", lambda e, pp=pp: e.tensor_scalar_mul(
                                ebias[:, pp:pp + 1], ngmx[:, pp:pp + 1], 0.125), inc="V",
                                guard=True)
                            w = [("V", vb2)]
                            if t_tr_done.get((hb, qi - 1)) or t_tr_done.get((1 - hb, 15)):
                                prev = t_tr_done.get((hb, qi - 1)) or t_tr_done.get((1 - hb, 15))
                                w.append(("T", prev))
                            s_exp_last[0] = rec.op("scalar", lambda e, L=L, pp=pp: e.activation(
                                ptm[:, 0:L],
                                ps_at[:].rearrange("p n f -> p (n f)")[:, 0:L],
                                AF.Exp, bias=ebias[:, pp:pp + 1], scale=0.125),
                                waits=w, inc="S")
                            ngrp = (qi + 1 + 3) // 4
                            for g in range(ngrp):
                                tj0 = 4 * g
                                cnt = min(4, qi + 1 - tj0)
                                bank = g % 2
                                tg = None
                                for j2 in range(cnt):
                                    tj = tj0 + j2
                                    w = [("S", s_exp_last[0])] if j2 == 0 else []
                                    if j2 == 0 and s_prT_last[0]:
                                        w.append(("S", s_prT_last[0]))
                                    if j2 == 0 and trc_evac[0]:
                                        w.append(("V", trc_evac[0]))
                                    tg = rec.op("tensor", lambda e, bank=bank, j2=j2, tj=tj:
                                        e.transpose(ps_tr2[:, bank, 128 * j2:128 * (j2 + 1)],
                                                    ptm[:, 128 * tj:128 * (tj + 1)],
                                                    ident[:]),
                                        waits=w, inc="T" if j2 == cnt - 1 else None)
                                s_prT_last[0] = rec.op("scalar",
                                    lambda e, bank=bank, tj0=tj0, cnt=cnt, qt=qt: e.copy(
                                        prTb[:, tj0:tj0 + cnt, 128 * qt:128 * (qt + 1)],
                                        ps_tr2[:, bank, 0:128 * cnt].rearrange(
                                            "p (j f) -> p j f", f=128)),
                                    waits=[("T", tg)], inc="S")
                            t_tr_done[(hb, qi)] = C["T"]
                            if qi + 1 < 4 * sp + 4:
                                v_przero[0] = rec.op("vector",
                                    lambda e, qi=qi, sp=sp, qt=qt: e.memset(
                                        prTb[:, qi + 1:4 * sp + 4,
                                             128 * qt:128 * (qt + 1)], 0.0),
                                    waits=[("T", t_tr_done[(hb, qi)])], inc="V")
                        nt = 4 * sp + 4
                        tpv = None
                        for tj in range(nt):
                            w = []
                            if tj == 0:
                                w = [("S", s_prT_last[0])]
                                if v_przero[0]:
                                    w.append(("V", v_przero[0]))
                                if v_pv_evac[0]:
                                    w.append(("V", v_pv_evac[0]))
                            tpv = rec.op("tensor", lambda e, hb=hb, tj=tj, nt=nt: e.matmul(
                                ps_pv[0:65, 0, :], Vau[:, hb, tj, :], prTb[:, tj, :],
                                start=(tj == 0), stop=(tj == nt - 1)),
                                waits=w, inc="T" if tj == nt - 1 else None)
                        slot = hb
                        if hb == 0:
                            v1 = rec.op("vector", lambda e, r=r, sp=sp: e.tensor_copy(
                                acT[0:64, r, 512 * sp:512 * (sp + 1)], ps_pv[0:64, 0, :]),
                                waits=[("T", tpv)], inc="V")
                        else:
                            w = [("T", tpv)]
                            if q_atmp[0]:
                                w.append(("QA", q_atmp[0]))
                            v1 = rec.op("vector", lambda e: e.tensor_copy(
                                atmp[:, 0, :], ps_pv[0:64, 0, :]), waits=w, inc="V")
                            q_atmp[0] = rec.op("sync", lambda e, r=r, sp=sp: e.dma_start(
                                out=acT[64:128, r, 512 * sp:512 * (sp + 1)],
                                in_=atmp[:, 0, :]), waits=[("V", v1)], inc="QA", n=16)
                        w = [("QR%d" % slot, q_rss[slot])] if q_rss[slot] else []
                        v2 = rec.op("vector", lambda e, slot=slot: e.tensor_copy(
                            rss[64:65, slot, :], ps_pv[64:65, 0, :]), waits=w, inc="V")
                        q_rss[slot] = rec.op("sync", lambda e, h=h, sp=sp, slot=slot:
                            e.dma_start(
                                out=rsums[h:h + 1, 512 * sp:512 * (sp + 1)],
                                in_=rss[64:65, slot, :]),
                            waits=[("V", v2)], inc="QR%d" % slot, n=16)
                        v_pv_evac[0] = v2

            q_attn = C["Q"]
            v_attn = C["V"]
            t_attn = C["T"]
            s_attn = C["S"]
            attn_dma_done = [("QR0", C["QR0"]), ("QR1", C["QR1"]), ("QA", C["QA"])]
            if "dbg_rs" in dbg_t:
                rec.op("sync", lambda e: e.dma_start(out=dbg_t["dbg_rs"], in_=rsums[:]),
                       waits=attn_dma_done, inc="QO", n=16)

            stCa.__exit__(None, None, None)

            stC2 = contextlib.ExitStack()
            with stC2:
                wost = stC2.enter_context(nc.sbuf_tensor("wost", [128, KC, D], BF16))
                woT = stC2.enter_context(nc.sbuf_tensor("woT", [128, KC, D], BF16))
                recf = stC2.enter_context(nc.sbuf_tensor("recf", [NH, S], F32))
                recb = stC2.enter_context(nc.sbuf_tensor("recb", [2, 2, S], BF16))
                trc2_ps = stC2.enter_context(nc.psum_tensor("trc2_ps", [128, 1, 1024], BF16))
                rep_ps = stC2.enter_context(nc.psum_tensor("rep_ps", [128, 2, 512], F32))
                op_ps = stC2.enter_context(nc.psum_tensor("op_ps", [128, 2, D], F32))
                log_space("phase C2")

                end_attn = [("T", t_attn), ("S", s_attn), ("V", v_attn)] + attn_dma_done
                g_wo = rec.op("gpsimd", lambda e: e.dma_start(
                    out=wost[:], in_=t["wo"].rearrange("(k p) d2 -> p k d2", p=128)),
                    waits=end_attn, inc="G", n=16)
                trc2_evac = [0]
                v_wot = {}
                for rr in range(KC):
                    for k in range(KC):
                        idx = rr * KC + k
                        w = []
                        if idx == 0:
                            w = [("G", g_wo), ("S", s_attn), ("V", v_attn)]
                        if trc2_evac[0]:
                            w.append(("V", trc2_evac[0]))
                        tp = rec.op("tensor", lambda e, rr=rr, k=k: e.transpose(
                            trc2_ps[:, 0, 0:128], wost[:, rr, 128 * k:128 * (k + 1)],
                            ident[:]), waits=w, inc="T")
                        trc2_evac[0] = v_wot[idx] = rec.op("vector",
                            lambda e, rr=rr, k=k: e.tensor_copy(
                                woT[:, k, 128 * rr:128 * (rr + 1)], trc2_ps[:, 0, 0:128]),
                            waits=[("T", tp)], inc="V")
                v_wot_done = C["V"]

                v_rec = rec.op("vector", lambda e: e.reciprocal(recf[:], rsums[:]),
                               waits=attn_dma_done, inc="V")
                v_norm = {}
                q_recb = {}
                for rr in range(KC):
                    sl = rr % 2
                    w = [("V", v_rec)]
                    if rr >= 2:
                        w.append(("T", q_recb[rr - 2][1]))
                    for h2 in range(2):
                        q_recb[rr] = (rec.op("gpsimd",
                            lambda e, rr=rr, sl=sl, h2=h2: e.dma_start(
                                out=recb[h2:h2 + 1, sl, :], in_=recf[2 * rr + h2:2 * rr + h2 + 1, :]),
                            waits=w if h2 == 0 else [], inc="QB%d" % sl, n=16), None)
                    q_recb[rr] = (q_recb[rr][0], None)
                    for n in range(4):
                        idx = rr * 4 + n
                        bank = idx % 2
                        w = [("QB%d" % sl, q_recb[rr][0])]
                        if idx == 0:
                            w += [("S", s_attn), ("V", v_attn)] + attn_dma_done
                        if idx >= 2:
                            w.append(("V", v_norm[idx - 2]))
                        tp = rec.op("tensor", lambda e, rr=rr, n=n, bank=bank, sl=sl: e.matmul(
                            rep_ps[:, bank, :], sel2[:],
                            recb[0:2, sl, 512 * n:512 * (n + 1)], start=True, stop=True),
                            waits=w, inc="T")
                        q_recb[rr] = (q_recb[rr][0], tp)
                        v_norm[idx] = rec.op("vector", lambda e, rr=rr, n=n, bank=bank:
                            e.tensor_tensor(
                                acT[:, rr, 512 * n:512 * (n + 1)],
                                acT[:, rr, 512 * n:512 * (n + 1)],
                                rep_ps[:, bank, :], ALU.mult),
                            waits=[("T", tp)], inc="V")
                v_norm_done = C["V"]
                if "dbg_attn" in dbg_t:
                    rec.op("sync", lambda e: e.dma_start(out=dbg_t["dbg_attn"], in_=acT[:]),
                           waits=[("V", v_norm_done)], inc="QO", n=16)

                v_Y = {}
                for ti in range(TT):
                    pr = ti % 2
                    for k in range(KC):
                        for half in range(2):
                            w = []
                            if k == 0 and half == 0:
                                w = [("V", v_norm_done), ("V", v_wot_done)]
                                if ti >= 2:
                                    w.append(("V", v_Y[ti - 2]))
                            to = rec.op("tensor", lambda e, ti=ti, k=k, half=half, pr=pr:
                                e.matmul(op_ps[:, pr, 512 * half:512 * (half + 1)],
                                         acT[:, k, 128 * ti:128 * (ti + 1)],
                                         woT[:, k, 512 * half:512 * (half + 1)],
                                         start=(k == 0), stop=(k == KC - 1)),
                                waits=w,
                                inc="T" if (k == KC - 1 and half == 1) else None)
                    v_Y[ti] = rec.op("vector", lambda e, ti=ti, pr=pr: e.tensor_tensor(
                        Y[:, ti, :], op_ps[:, pr, :], Y[:, ti, :], ALU.add),
                        waits=[("T", to), ("Q", q_xy_tot)], inc="V")
                v_c2 = C["V"]
                t_c2 = C["T"]

        stD = contextlib.ExitStack()
        with stD:
            hT = stD.enter_context(nc.sbuf_tensor("hT", [128, 2, 4, S], BF16))
            upw = stD.enter_context(nc.sbuf_tensor("upw_s", [128, 2, KC, IM], BF16))
            dnw = stD.enter_context(nc.sbuf_tensor("dnw_s", [128, 2, 4, D], BF16))
            rsc = stD.enter_context(nc.sbuf_tensor("rsc", [128, 2, 512], F32))
            h_ps = stD.enter_context(nc.psum_tensor("h_ps", [128, 2, 512], F32))
            d_ps = stD.enter_context(nc.psum_tensor("d_ps", [128, 2, D], F32))
            log_space("phase D")

            t_up_done, t_dn_done = {}, {}
            g_up, g_dn = {}, {}
            s_h = {}
            hps_free = {0: 0, 1: 0}
            dps_free = {0: 0, 1: 0}
            s_first = {}
            v_stt = {}

            def up_src(u):
                if u < NE:
                    return t["upw"][u].rearrange("(k p) i2 -> p k i2", p=128)
                j = u - NE
                return t["shup"].rearrange("(k p) i2 -> p k i2", p=128)[
                    :, :, IM * j:IM * (j + 1)]

            def dn_src(u):
                if u < NE:
                    return t["downw"][u].rearrange("(kk p) m -> p kk m", p=128)
                j = u - NE
                return t["shdn"][IM * j:IM * (j + 1), :].rearrange(
                    "(kk p) m -> p kk m", p=128)

            for u in range(NU):
                pu = u % 2
                w = [("T", t_c2, ), ("V", v_c2)] if u < 2 else [("T", t_up_done[u - 2])]
                g_up[u] = rec.op("gpsimd", lambda e, u=u, pu=pu: e.dma_start(
                    out=upw[:, pu, :, :], in_=up_src(u)), waits=w, inc="GU%d" % pu, n=16)
                w = [] if u < 2 else [("T", t_dn_done[u - 2])]
                g_dn[u] = rec.op("gpsimd", lambda e, u=u, pu=pu: e.dma_start(
                    out=dnw[:, pu, :, :], in_=dn_src(u)), waits=w, inc="GD%d" % pu, n=16)

                def emit_up(cc, u=u, pu=pu):
                    for j in range(4):
                        sl = j % 2
                        t_up = None
                        for k in range(KC):
                            w = []
                            if k == 0:
                                w = [("GU%d" % pu, g_up[u]), ("S", hps_free[sl])]
                            t_up = rec.op("tensor", lambda e, pu=pu, j=j, k=k, cc=cc, sl=sl:
                                e.matmul(h_ps[:, sl, :],
                                         upw[:, pu, k, 128 * j:128 * (j + 1)],
                                         xnT[:, k, 512 * cc:512 * (cc + 1)],
                                         start=(k == 0), stop=(k == KC - 1)),
                                waits=w, inc="T" if k == KC - 1 else None)
                        w = [("T", t_up)]
                        if (u, cc, j) == (u, 0, 0) and cc == 0 and j == 0 and u >= 2:
                            w.append(("T", t_dn_done[u - 2]))
                        sr = rec.op("scalar", lambda e, sl=sl: e.activation(
                            rsc[:, sl, :], h_ps[:, sl, :], AF.Relu), waits=w, inc="S")
                        hps_free[sl] = sr
                        s_h[(u, cc, j)] = rec.op("scalar", lambda e, pu=pu, j=j, cc=cc, sl=sl:
                            e.activation(hT[:, pu, j, 512 * cc:512 * (cc + 1)],
                                         rsc[:, sl, :], AF.Square), inc="S", guard=True)

                def emit_dn(cc, u=u, pu=pu):
                    for ti in range(4):
                        tt = 4 * cc + ti
                        sl = tt % 2
                        t_dn = None
                        for kk in range(4):
                            for half in range(2):
                                w = []
                                if kk == 0 and half == 0:
                                    w = [("GD%d" % pu, g_dn[u]), ("S", s_h[(u, cc, 3)]),
                                         ("V", dps_free[sl])]
                                t_dn = rec.op("tensor",
                                    lambda e, pu=pu, kk=kk, tt=tt, half=half, sl=sl:
                                    e.matmul(d_ps[:, sl, 512 * half:512 * (half + 1)],
                                             hT[:, pu, kk, 128 * tt:128 * (tt + 1)],
                                             dnw[:, pu, kk, 512 * half:512 * (half + 1)],
                                             start=(kk == 0), stop=(kk == 3)),
                                    waits=w,
                                    inc="T" if (kk == 3 and half == 1) else None)
                        if u < NE:
                            v_stt[(u, tt)] = rec.op("vector",
                                lambda e, u=u, tt=tt, sl=sl: e.scalar_tensor_tensor(
                                    Y[:, tt, :], d_ps[:, sl, :],
                                    ctile[:, tt, u:u + 1], Y[:, tt, :],
                                    ALU.mult, ALU.add),
                                waits=[("T", t_dn)], inc="V")
                        else:
                            v_stt[(u, tt)] = rec.op("vector",
                                lambda e, tt=tt, sl=sl: e.tensor_tensor(
                                    Y[:, tt, :], d_ps[:, sl, :], Y[:, tt, :], ALU.add),
                                waits=[("T", t_dn)], inc="V")
                        dps_free[sl] = v_stt[(u, tt)]

                emit_up(0)
                emit_up(1)
                emit_dn(0)
                emit_up(2)
                emit_dn(1)
                emit_up(3)
                t_up_done[u] = C["T"]
                emit_dn(2)
                emit_dn(3)
                t_dn_done[u] = C["T"]

            if "dbg_Y" in dbg_t:
                rec.op("sync", lambda e: e.dma_start(out=dbg_t["dbg_Y"], in_=Y[:]),
                       waits=[("V", C["V"])], inc="QO", n=16)
            for tt in range(TT):
                rec.op("sync", lambda e, tt=tt: e.dma_start(
                    out=t["y"][128 * tt:128 * (tt + 1), :], in_=Y[:, tt, :]),
                    waits=[("V", v_stt[(NU - 1, tt)])], inc="QO", n=16)
            rec.wait("sync", [("QO", C["QO"]), ("GO", C["GO"])])
            log_space("end")

        sems = {}
        sem_stack = contextlib.ExitStack()
        with sem_stack:
            for name in ("Q", "QO", "G", "GO", "T", "V", "S",
                         "QX0", "QX1", "QX2", "QX3", "QR0", "QR1", "QA",
                         "QB0", "QB1", "GU0", "GU1", "GD0", "GD1"):
                sems[name] = sem_stack.enter_context(nc.semaphore(name))
            rec.replay(nc, sems)

    return nc


_INPUT_KEYS = dict(
    x="L0_attn_x", wqkv="L0_attn_w_qkv", wo="L0_attn_w_o",
    rw="L0_ffn_router_w", rb="L0_ffn_correction_bias",
    upw="L0_ffn_up_w", downw="L0_ffn_down_w",
    shup="L0_ffn_shared_up_w", shdn="L0_ffn_shared_down_w",
)


def kernel(**inputs):
    arrs = {k: np.ascontiguousarray(np.asarray(inputs[v], dtype=np.float32))
            for k, v in _INPUT_KEYS.items()}
    nc = build_nc()
    in_maps = []
    for c in range(8):
        m = {k: arrs[k] for k in arrs if k != "x"}
        m["x"] = np.ascontiguousarray(arrs["x"][c])
        in_maps.append(m)
    res = run_bass_kernel_spmd(nc, in_maps, core_ids=list(range(8)))
    return np.stack([np.asarray(r["y"]) for r in res.results], axis=0)
